# version 51
# baseline (speedup 1.0000x reference)
import numpy as np
import ml_dtypes
from contextlib import ExitStack

import concourse.bass as bass
import concourse.mybir as mybir
from concourse.bass_utils import run_bass_kernel_spmd

F32 = mybir.dt.float32
BF16 = mybir.dt.bfloat16
I32 = mybir.dt.int32
AF = mybir.ActivationFunctionType
ALU = mybir.AluOpType
AX = mybir.AxisListType

class Ser:

    def __init__(self, eng, sem):
        self._e = eng
        self._s = sem
        self._n = 0
        self._last = None

    def wait_ge(self, sem, val):
        return self._e.wait_ge(sem, val)

    def __getattr__(self, name):
        fn = getattr(self._e, name)

        def call(*a, _inc=None, **kw):
            if self._last is not None:
                self._e.wait_ge(*self._last)
            inst = fn(*a, **kw)
            if _inc is not None:
                sem, tgt = _inc
                inst.then_inc(sem, 1)
                self._last = (sem, tgt)
            else:
                self._n += 1
                inst.then_inc(self._s, 1)
                self._last = (self._s, self._n)
            return inst

        return call


NCORES = 8
DM = 1024
TPC = 32768 // NCORES
NT = TPC // 128
KC = 8


def _hadamard_bf16_packed():
    h = np.array([[1.0]], dtype=np.float64)
    while h.shape[0] < DM // 2:
        h = np.block([[h, h], [h, -h]])
    h = (h / 32.0).astype(np.float32)
    hp = h.reshape(4, 128, DM // 2).transpose(1, 0, 2).reshape(128, 4 * (DM // 2))
    return hp.astype(ml_dtypes.bfloat16)


def build_nc(nt=NT):
    tpc = nt * 128
    nc = bass.Bass("TRN2", target_bir_lowering=False)
    x_hbm = nc.dram_tensor("x", [tpc, DM], F32, kind="ExternalInput").ap()
    w1_hbm = nc.dram_tensor("w1", [DM, DM], F32, kind="ExternalInput").ap()
    w23_hbm = nc.dram_tensor("w23", [DM, 32], F32, kind="ExternalInput").ap()
    b1_hbm = nc.dram_tensor("b1r", [1, DM], F32, kind="ExternalInput").ap()
    b23_hbm = nc.dram_tensor("b23r", [1, 32], F32, kind="ExternalInput").ap()
    wf_hbm = nc.dram_tensor("wf", [DM, DM], F32, kind="ExternalInput").ap()
    h_hbm = nc.dram_tensor("hpk", [128, 4 * (DM // 2)], BF16, kind="ExternalInput").ap()
    identbf_hbm = nc.dram_tensor("identbf", [128, 128], BF16, kind="ExternalInput").ap()
    identf_hbm = nc.dram_tensor("identf", [128, 128], F32, kind="ExternalInput").ap()
    onesbf_hbm = nc.dram_tensor("onesbf", [1, 128], BF16, kind="ExternalInput").ap()
    onesf_hbm = nc.dram_tensor("onesf", [1, 128], F32, kind="ExternalInput").ap()
    c160_hbm = nc.dram_tensor("c160", [128, 1], F32, kind="ExternalInput").ap()
    nc160_hbm = nc.dram_tensor("nc160", [128, 1], F32, kind="ExternalInput").ap()
    out_hbm = nc.dram_tensor("out", [tpc, DM], F32, kind="ExternalOutput").ap()

    with ExitStack() as es:
        S = lambda n, sh, dt: es.enter_context(nc.sbuf_tensor(n, sh, dt))
        SEM = lambda n: es.enter_context(nc.semaphore(n))

        W1s = S("W1s", [128, KC * DM], BF16)
        Hs = S("Hs", [128, 4 * (DM // 2)], BF16)
        W23s = S("W23s", [128, KC * 32], BF16)
        ternT = S("ternT", [128, KC * DM], BF16)
        b1s = S("b1s", [1, DM], BF16)
        b23s = S("b23s", [1, 32], BF16)
        onesbf = S("onesbf_s", [1, 128], BF16)
        onesf = S("onesf_s", [1, 128], F32)
        identbf = S("identbf_s", [128, 128], BF16)
        identf = S("identf_s", [128, 128], F32)
        c160s = S("c160_s", [128, 1], F32)
        nc160s = S("nc160_s", [128, 1], F32)
        wfstage = [S(f"wfst{i}", [128, DM], F32) for i in range(3)]
        cstage = [S(f"cst{i}", [128, DM], BF16) for i in range(2)]
        ternbf = [S(f"tbf{i}", [128, DM], BF16) for i in range(3)]
        sw_all = S("sw_all", [128, KC], F32)
        swh_all = S("swh_all", [128, KC], F32)
        nswh_all = S("nswh_all", [128, KC], F32)
        absum = S("absum", [128, 1], F32)
        xbuf = [S(f"xb{i}", [128, DM], F32) for i in range(11)]
        xnbf = [S(f"xnb{i}", [128, DM], BF16) for i in range(8)]
        xnT = [S(f"xnT{i}", [128, DM], BF16) for i in range(3)]
        uT2 = [S(f"uT{i}", [128, 512], BF16) for i in range(4)]
        vT2 = [S(f"vT{i}", [128, 512], BF16) for i in range(4)]
        xqm = S("xqm", [128, DM], BF16)
        xqbf = S("xqb", [128, DM], BF16)
        xqT = [S(f"xqT{i}", [128, DM], BF16) for i in range(2)]
        esb = S("esb", [128, DM], F32)
        l2 = S("l2", [128, DM], F32)
        s2 = [S(f"s2_{i}", [128, DM], F32) for i in range(3)]
        delta = [S(f"del{i}", [128, DM], BF16) for i in range(6)]
        o3s = [S(f"o3s{i}", [128, DM], F32) for i in range(2)]
        t1buf = S("t1b", [128, DM], F32)
        t3buf = S("t3b", [128, DM], F32)
        t4buf = S("t4b", [128, DM], F32)
        obuf = [S(f"ob{i}", [128, DM], F32) for i in range(2)]
        dump_a = S("dumpa", [128, DM], BF16)
        prod = S("prod", [128, 16], F32)
        bcw = S("bcw", [128, 32], F32)
        mk1 = lambda n: [S(f"{n}{i}", [128, 1], F32) for i in range(2)]
        sumsq = mk1("ssq"); xsum = mk1("xsm"); mean = mk1("mean"); var = mk1("var")
        mk4 = lambda n: [S(f"{n}{i}", [128, 1], F32) for i in range(4)]
        rs = mk1("rs"); nmrs = mk1("nmrs"); sa = mk1("sa"); nsa = mk4("nsa")
        inv = mk1("inv"); dot = mk1("dot"); cc = mk4("cc")
        mx = S("mx", [128, 1], F32); mxc = S("mxc", [128, 1], F32); mxh = S("mxh", [128, 1], F32)
        msq = S("msq", [128, 1], F32); lnv = S("lnv", [128, 1], F32)

        psZ1 = es.enter_context(nc.psum_tensor("psZ1", [128, DM], F32))
        psXH = es.enter_context(nc.psum_tensor("psXH", [128, DM], F32))
        psO3x = [es.enter_context(nc.psum_tensor(f"psO3{i}", [128, DM], F32)) for i in range(2)]
        psO3 = psO3x[0]

        sLDa = [SEM(f"sLD{i}") for i in range(11)]
        sWS = SEM("sWS"); sWG = SEM("sWG"); sC160 = SEM("sC160")
        sWFa = [SEM(f"sWF{i}") for i in range(3)]
        sTXNa = [SEM(f"sTXN{i}") for i in range(3)]
        sTXQa = [SEM(f"sTXQ{i}") for i in range(2)]
        sSTa = [SEM(f"sST{i}") for i in range(2)]
        sSWD = SEM("sSWD")
        sPEZ = SEM("sPEZ"); sPEXH = SEM("sPEXH"); sPEO3 = SEM("sPEO3")
        sXNB = SEM("sXNB"); sEXP = SEM("sEXP"); sRND = SEM("sRND"); sSIG = SEM("sSIG")
        sSQ = SEM("sSQ"); sRS = SEM("sRS")
        sVAR = SEM("sVAR"); sNMRS = SEM("sNMRS"); sMX = SEM("sMX"); sO3S = SEM("sO3S")
        sXQB = SEM("sXQB"); sBCE = SEM("sBCE"); sVOUT = SEM("sVOUT")
        sTBF = SEM("sTBF"); sPTT = SEM("sPTT"); sTEV = SEM("sTEV"); sUVT = SEM("sUVT")
        sDT4 = SEM("sDT4"); sPEBC = SEM("sPEBC"); sPT1 = SEM("sPT1"); sPEXH1 = SEM("sPEXH1")
        sSELFD = SEM("sSELFD"); sSELFA = SEM("sSELFA"); sSELFG = SEM("sSELFG"); sSELFP = SEM("sSELFP")

        block = es.enter_context(nc.Block())

        W23s3 = W23s[:].rearrange("p (c n) -> p c n", c=KC)
        ternT3 = ternT[:].rearrange("p (c n) -> p c n", c=KC)
        psO3bf = psO3[:].bitcast(BF16)
        psO3bf3 = psO3bf[:, 0:DM].rearrange("p (c n) -> p c n", c=KC)

        @block.sync
        def _(sp):
            sp.dma_start(out=c160s[:], in_=c160_hbm[:]).then_inc(sC160, 16)
            sp.dma_start(out=nc160s[:], in_=nc160_hbm[:]).then_inc(sC160, 16)
            sp.dma_start(out=identbf[:], in_=identbf_hbm[:]).then_inc(sWS, 16)
            sp.dma_start(out=identf[:], in_=identf_hbm[:]).then_inc(sWS, 16)
            sp.dma_start(out=onesbf[:], in_=onesbf_hbm[:]).then_inc(sWS, 16)
            sp.dma_start(out=onesf[:], in_=onesf_hbm[:]).then_inc(sWS, 16)
            for t in range(min(5, nt)):
                sp.dma_start(out=xbuf[t % 11][:], in_=x_hbm[t * 128:(t + 1) * 128, :]).then_inc(sLDa[t % 11], 16)
            sp.dma_start(out=Hs[:], in_=h_hbm[:]).then_inc(sWS, 16)
            for c in range(-4, nt + 4):
                t = c + 5
                if 5 <= t < nt:
                    if t >= 11:
                        sp.wait_ge(sVOUT, t - 10)
                    sp.dma_start(out=xbuf[t % 11][:], in_=x_hbm[t * 128:(t + 1) * 128, :]).then_inc(sLDa[t % 11], 16)
                t = c + 2
                if 0 <= t < nt:
                    sp.wait_ge(sXNB, t + 1)
                    if t >= 3:
                        sp.wait_ge(sPEBC, t - 1)
                        sp.wait_ge(sUVT, t - 2)
                    sp.dma_start(out=xnT[t % 3][:].rearrange("p (c n) -> p c n", c=KC),
                                 in_=xnbf[t % 8][:].rearrange("p (c n) -> p c n", c=KC),
                                 transpose=True).then_inc(sTXNa[t % 3], 16)
                t = c - 1
                if 0 <= t < nt:
                    sp.wait_ge(sXQB, t + 1)
                    if t >= 2:
                        sp.wait_ge(sPEO3, t - 1)
                    sp.dma_start(out=xqT[t % 2][:].rearrange("p (c n) -> p c n", c=KC),
                                 in_=xqbf[:].rearrange("p (c n) -> p c n", c=KC),
                                 transpose=True).then_inc(sTXQa[t % 2], 16)

        @block.gpsimd
        def _(g):
            g.dma_start(out=W23s3, in_=w23_hbm.rearrange("(c p) n -> p c n", p=128)).then_inc(sWG, 16)
            g.dma_start(out=b1s[:], in_=b1_hbm[:]).then_inc(sWG, 16)
            g.dma_start(out=b23s[:], in_=b23_hbm[:]).then_inc(sWG, 16)

            def w1_load(ic):
                g.dma_start(out=W1s[:, ic * DM:(ic + 1) * DM],
                            in_=w1_hbm[ic * 128:(ic + 1) * 128, :]).then_inc(sWG, 16)
            def wf_load(oc):
                if oc >= 3:
                    g.wait_ge(sTBF, oc - 2)
                g.dma_start(out=wfstage[oc % 3][:], in_=wf_hbm[oc * 128:(oc + 1) * 128, :]).then_inc(sWFa[oc % 3], 16)

            for ic in range(KC):
                w1_load(ic)
            for k in range(KC):
                wf_load(k)
            gs = Ser(g, sSELFG)
            for c in range(-4, nt + 4):
                tu = c + 1
                if 0 <= tu < nt:
                    g.wait_ge(sTXNa[tu % 3], 16 * (tu // 3 + 1))
                    if tu >= 4:
                        g.wait_ge(sPEXH, tu - 3)
                    xT_ = xnT[tu % 3]
                    gs.tensor_tensor(uT2[tu % 4][:], xT_[:, 0:512], xT_[:, 512:DM], ALU.add)
                    gs.tensor_tensor(vT2[tu % 4][:], xT_[:, 0:512], xT_[:, 512:DM],
                                     ALU.subtract, _inc=(sUVT, tu + 1))
                w2 = c - 4
                if 0 <= w2 < nt:
                    g.wait_ge(sDT4, w2 + 1)
                    g.wait_ge(sO3S, w2 + 1)
                    g.wait_ge(sLDa[w2 % 11], 16 * (w2 // 11 + 1))
                    if w2 >= 2:
                        g.wait_ge(sSTa[w2 % 2], 16 * ((w2 - 2) // 2 + 1))
                    gs.tensor_tensor(t4buf[:], t3buf[:], o3s[w2 % 2][:], ALU.mult)
                    gs.tensor_tensor(obuf[w2 % 2][:], xbuf[w2 % 11][:], t4buf[:], ALU.add, _inc=(sVOUT, w2 + 1))
                    g.wait_ge(sVOUT, w2 + 1)
                    g.dma_start(out=out_hbm[w2 * 128:(w2 + 1) * 128, :], in_=obuf[w2 % 2][:]).then_inc(sSTa[w2 % 2], 16)
            g.wait_ge(sSTa[0], 16 * ((nt + 1) // 2))
            if nt >= 2:
                g.wait_ge(sSTa[1], 16 * (nt // 2))

        @block.tensor
        def _(pe):
            def prep_tt(ocs):
                for oc in ocs:
                    pe.wait_ge(sTBF, oc + 1)
                    if oc >= 1:
                        pe.wait_ge(sTEV, oc)
                    last = None
                    for ic in range(KC):
                        last = pe.transpose(psO3bf[:, ic * 128 + 0:ic * 128 + 128],
                                            ternbf[oc % 3][:, ic * 128:(ic + 1) * 128], identbf[:])
                    last.then_inc(sPTT, 1)

            for c in range(-4, nt + 4):
                t = c
                if 0 <= t < nt:
                    if t == 0:
                        pe.wait_ge(sWG, 16 * 11)
                        pe.wait_ge(sWS, 16 * 5)
                        pe.wait_ge(sC160, 32)
                    pe.wait_ge(sTXNa[t % 3], 16 * (t // 3 + 1))
                    pe.wait_ge(sBCE, t + 1)
                    xT = xnT[t % 3]
                    for ic in range(KC):
                        lh = xT[:, ic * 128:(ic + 1) * 128]
                        pe.matmul(psZ1[:, 0:512], lh, W1s[:, ic * DM:ic * DM + 512], start=(ic == 0), stop=False)
                        pe.matmul(psZ1[:, 512:1024], lh, W1s[:, ic * DM + 512:(ic + 1) * DM], start=(ic == 0), stop=False)
                    pe.matmul(psZ1[:, 0:512], onesbf[:], b1s[:, 0:512], start=False, stop=True)
                    pe.matmul(psZ1[:, 512:1024], onesbf[:], b1s[:, 512:1024], start=False, stop=True).then_inc(sPEZ, 1)
                if c == 0:
                    prep_tt(range(0, 5))
                if c == 1:
                    prep_tt(range(5, KC))
                u = c - 2
                if 0 <= u < nt:
                    if u == 0:
                        pe.wait_ge(sTEV, KC)
                    pe.wait_ge(sTXQa[u % 2], 16 * (u // 2 + 1))
                    if u >= 2:
                        pe.wait_ge(sO3S, u - 1)
                        pe.wait_ge(sSIG, u - 1)
                    ps = psO3x[u % 2]
                    qT = xqT[u % 2]
                    last = None
                    for ic in range(KC):
                        lh = qT[:, ic * 128:(ic + 1) * 128]
                        pe.matmul(ps[:, 0:512], lh, ternT[:, ic * DM:ic * DM + 512], start=(ic == 0), stop=(ic == KC - 1))
                        last = pe.matmul(ps[:, 512:1024], lh, ternT[:, ic * DM + 512:(ic + 1) * DM], start=(ic == 0), stop=(ic == KC - 1))
                    last.then_inc(sPEO3, 1)
                tn = c + 1
                if 0 <= tn < nt:
                    if tn == 0:
                        pe.wait_ge(sWG, 16 * 11)
                        pe.wait_ge(sWS, 16 * 5)
                        pe.wait_ge(sC160, 32)
                    else:
                        pe.wait_ge(sEXP, tn)
                    pe.wait_ge(sTXNa[tn % 3], 16 * (tn // 3 + 1))
                    xTn = xnT[tn % 3]
                    for ic in range(KC):
                        pe.matmul(psZ1[:, 0:32], xTn[:, ic * 128:(ic + 1) * 128],
                                  W23s[:, ic * 32:(ic + 1) * 32], start=(ic == 0), stop=False)
                    pe.matmul(psZ1[:, 0:32], onesbf[:], b23s[:], start=False, stop=True).then_inc(sPEBC, 1)
                if 0 <= t < nt:
                    if t >= 1:
                        pe.wait_ge(sRND, t)
                    pe.wait_ge(sUVT, t + 1)
                    last = None
                    for ic in range(4):
                        pe.matmul(psXH[:, 0:512], uT2[t % 4][:, ic * 128:(ic + 1) * 128],
                                  Hs[:, ic * 512:(ic + 1) * 512], start=(ic == 0), stop=(ic == 3))
                        last = pe.matmul(psXH[:, 512:1024], vT2[t % 4][:, ic * 128:(ic + 1) * 128],
                                         Hs[:, ic * 512:(ic + 1) * 512], start=(ic == 0), stop=(ic == 3))
                    last.then_inc(sPEXH, 1)

        @block.scalar
        def _(ac):
            ac = Ser(ac, sSELFA)

            def prep_evac(ocs):
                for oc in ocs:
                    ac.wait_ge(sPTT, oc + 1)
                    ac.copy(ternT3[:, :, oc * 128:(oc + 1) * 128], psO3bf3, _inc=(sTEV, oc + 1))

            ac.wait_ge(sC160, 32)
            for c in range(-4, nt + 4):
                v4 = c + 4
                if 0 <= v4 < nt:
                    ac.wait_ge(sLDa[v4 % 11], 16 * (v4 // 11 + 1))
                    ac.activation(dump_a[:], xbuf[v4 % 11][:], AF.Square,
                                  accum_out=sumsq[v4 % 2][:], _inc=(sSQ, v4 + 1))
                v3 = c + 3
                if 0 <= v3 < nt:
                    ac.wait_ge(sVAR, v3 + 1)
                    ac.activation(lnv[:], var[v3 % 2][:], AF.Ln)
                    ac.activation(rs[v3 % 2][:], lnv[:], AF.Exp, scale=-0.5, _inc=(sRS, v3 + 1))
                if c - 5 >= 1:
                    ac.wait_ge(sVOUT, c - 5)
                v = c + 2
                if 0 <= v < nt:
                    ac.wait_ge(sNMRS, v + 1)
                    ac.activation(xnbf[v % 8][:], xbuf[v % 11][:], AF.Identity,
                                  bias=nmrs[v % 2][:], scale=rs[v % 2][:], _inc=(sXNB, v + 1))
                t = c
                if 0 <= t < nt:
                    ac.wait_ge(sPEZ, t + 1)
                    ac.activation(esb[:], psZ1[:], AF.Exp, _inc=(sEXP, t + 1))
                    ac.activation(delta[t % 6][:], esb[:], AF.Ln, bias=1.0)
                u = c - 1
                if 0 <= u < nt:
                    ac.wait_ge(sMX, u + 1)
                    ac.activation(xqm[:], psXH[:], AF.Identity, scale=inv[u % 2][:],
                                  bias=c160s[:], _inc=(sRND, u + 1))
                    if u >= 1:
                        ac.wait_ge(sTXQa[(u - 1) % 2], 16 * ((u - 1) // 2 + 1))
                    ac.activation(xqbf[:], xqm[:], AF.Identity, bias=nc160s[:], _inc=(sXQB, u + 1))
                w_ = c - 3
                if 0 <= w_ < nt:
                    ac.wait_ge(sO3S, w_ + 1)
                    ac.activation(esb[:], psO3x[w_ % 2][:], AF.Exp, scale=nsa[w_ % 4][:])
                    ac.activation(l2[:], esb[:], AF.Ln, bias=1.0)
                    ac.activation(s2[w_ % 3][:], l2[:], AF.Exp, scale=-1.0, _inc=(sSIG, w_ + 1))
                if c == 0:
                    prep_evac(range(0, 5))
                if c == 1:
                    prep_evac(range(5, KC))

        @block.vector
        def _(dv):
            dv = Ser(dv, sSELFD)

            def prep_pre(ocs):
                for oc in ocs:
                    dv.wait_ge(sWFa[oc % 3], 16 * (oc // 3 + 1))
                    dv.tensor_reduce(absum[:], wfstage[oc % 3][:], AX.X, ALU.add, apply_absolute_value=True)
                    dv.tensor_scalar(sw_all[:, oc:oc + 1], absum[:], 1.0 / DM, 1e-5, ALU.mult, ALU.max)
                    dv.tensor_single_scalar(swh_all[:, oc:oc + 1], sw_all[:, oc:oc + 1], 0.5, ALU.mult)
                    dv.tensor_single_scalar(nswh_all[:, oc:oc + 1], sw_all[:, oc:oc + 1], -0.5, ALU.mult)

            def prep_cp(ocs):
                for oc in ocs:
                    dv.tensor_single_scalar(cstage[oc % 2][:], wfstage[oc % 3][:],
                                            nswh_all[:, oc:oc + 1], ALU.is_lt)
                    if oc >= 3:
                        dv.wait_ge(sPTT, oc - 2)
                    dv.scalar_tensor_tensor(ternbf[oc % 3][:], wfstage[oc % 3][:], swh_all[:, oc:oc + 1],
                                            cstage[oc % 2][:], ALU.is_gt, ALU.subtract)
                    dv.tensor_single_scalar(ternbf[oc % 3][:], ternbf[oc % 3][:], sw_all[:, oc:oc + 1],
                                            ALU.mult, _inc=(sTBF, oc + 1))

            for c in range(-4, nt + 4):
                if c == 0:
                    prep_cp([2]); prep_pre([4]); prep_cp([3]); prep_pre([5]); prep_cp([4]); prep_pre([6])
                if c == 1:
                    prep_cp([5]); prep_pre([7]); prep_cp([6]); prep_cp([7])
                u = c - 1
                if 0 <= u < nt:
                    dv.wait_ge(sPEXH, u + 1)
                    dv.tensor_reduce(mx[:], psXH[:], AX.X, ALU.max, apply_absolute_value=True)
                    dv.tensor_single_scalar(mxc[:], mx[:], 1e-5, ALU.max)
                    dv.tensor_single_scalar(sa[u % 2][:], mxc[:], 1.0 / 7.0, ALU.mult)
                    dv.tensor_single_scalar(nsa[u % 4][:], sa[u % 2][:], -1.0, ALU.mult)
                    dv.reciprocal(inv[u % 2][:], sa[u % 2][:])
                    dv.tensor_tensor(cc[u % 4][:], dot[u % 2][:], sa[u % 2][:], ALU.mult, _inc=(sMX, u + 1))
                v4 = c + 4
                if 0 <= v4 < nt:
                    dv.wait_ge(sLDa[v4 % 11], 16 * (v4 // 11 + 1))
                    dv.tensor_reduce(xsum[v4 % 2][:], xbuf[v4 % 11][:], AX.X, ALU.add)
                    dv.wait_ge(sSQ, v4 + 1)
                    dv.tensor_single_scalar(mean[v4 % 2][:], xsum[v4 % 2][:], 1.0 / DM, ALU.mult)
                    dv.tensor_tensor(msq[:], mean[v4 % 2][:], mean[v4 % 2][:], ALU.mult)
                    dv.tensor_single_scalar(msq[:], msq[:], -1e-5, ALU.add)
                    dv.scalar_tensor_tensor(var[v4 % 2][:], sumsq[v4 % 2][:], 1.0 / DM, msq[:],
                                            ALU.mult, ALU.subtract, _inc=(sVAR, v4 + 1))
                v3 = c + 3
                if 0 <= v3 < nt:
                    dv.wait_ge(sRS, v3 + 1)
                    dv.scalar_tensor_tensor(nmrs[v3 % 2][:], mean[v3 % 2][:], -1.0, rs[v3 % 2][:],
                                            ALU.mult, ALU.mult, _inc=(sNMRS, v3 + 1))
                td = c + 1
                if 0 <= td < nt:
                    dv.wait_ge(sPEBC, td + 1)
                    dv.tensor_single_scalar(bcw[:], psZ1[:, 0:32], 0.0, ALU.add, _inc=(sBCE, td + 1))
                    dv.tensor_tensor(prod[:], bcw[:, 0:16], bcw[:, 16:32], ALU.mult)
                    dv.tensor_reduce(dot[td % 2][:], prod[:], AX.X, ALU.add)

                w_ = c - 3
                if 0 <= w_ < nt:
                    dv.wait_ge(sPEO3, w_ + 1)
                    if w_ >= 2:
                        dv.wait_ge(sVOUT, w_ - 1)
                    dv.tensor_copy(o3s[w_ % 2][:], psO3x[w_ % 2][:], _inc=(sO3S, w_ + 1))
                w2 = c - 4
                if 0 <= w2 < nt:
                    dv.wait_ge(sSIG, w2 + 1)
                    dv.tensor_tensor(t1buf[:], delta[w2 % 6][:], s2[w2 % 3][:], ALU.mult)
                    if w2 >= 1:
                        dv.wait_ge(sVOUT, w2)
                    dv.scalar_tensor_tensor(t3buf[:], xnbf[w2 % 8][:], cc[w2 % 4][:], t1buf[:],
                                            ALU.mult, ALU.mult, _inc=(sDT4, w2 + 1))
                if c == -1:
                    prep_pre([0]); prep_pre([1]); prep_cp([0]); prep_pre([2]); prep_cp([1]); prep_pre([3])

    return nc


_CACHE = {}


def kernel(L0_attn_x, L0_attn_W1, L0_attn_b1, L0_attn_W2, L0_attn_b2,
           L0_attn_W3, L0_attn_b3, L0_attn_A, L0_ffn_weight):
    del L0_attn_A
    x = np.ascontiguousarray(np.asarray(L0_attn_x, dtype=np.float32).reshape(32768, DM))
    W1 = np.ascontiguousarray(np.asarray(L0_attn_W1, dtype=np.float32))
    W23 = np.ascontiguousarray(np.concatenate(
        [np.asarray(L0_attn_W2, dtype=np.float32), np.asarray(L0_attn_W3, dtype=np.float32)], axis=1))
    b1r = np.ascontiguousarray(np.asarray(L0_attn_b1, dtype=np.float32).reshape(1, DM))
    b23r = np.ascontiguousarray(np.concatenate(
        [np.asarray(L0_attn_b2, dtype=np.float32), np.asarray(L0_attn_b3, dtype=np.float32)]).reshape(1, 32))
    wf = np.ascontiguousarray(np.asarray(L0_ffn_weight, dtype=np.float32))

    if "nc" not in _CACHE:
        _CACHE["nc"] = build_nc()
        _CACHE["hpk"] = _hadamard_bf16_packed()
        _CACHE["identbf"] = np.eye(128, dtype=ml_dtypes.bfloat16)
        _CACHE["identf"] = np.eye(128, dtype=np.float32)
        _CACHE["onesbf"] = np.ones((1, 128), dtype=ml_dtypes.bfloat16)
        _CACHE["onesf"] = np.ones((1, 128), dtype=np.float32)
        _CACHE["c160"] = np.full((128, 1), 160.0, dtype=np.float32)
        _CACHE["nc160"] = np.full((128, 1), -160.0, dtype=np.float32)
    nc = _CACHE["nc"]

    common = dict(w1=W1, w23=W23, b1r=b1r, b23r=b23r, wf=wf, hpk=_CACHE["hpk"],
                  identbf=_CACHE["identbf"], identf=_CACHE["identf"],
                  onesbf=_CACHE["onesbf"], onesf=_CACHE["onesf"], c160=_CACHE["c160"],
                  nc160=_CACHE["nc160"])
    in_maps = [dict(common, x=np.ascontiguousarray(x[i * TPC:(i + 1) * TPC])) for i in range(NCORES)]
    res = run_bass_kernel_spmd(nc, in_maps, core_ids=list(range(NCORES)))
    out = np.concatenate([np.asarray(r["out"]) for r in res.results], axis=0)
    return out.reshape(2, 16384, DM)


# version 52
# speedup vs baseline: 1.0156x; 1.0156x over previous
import numpy as np
import ml_dtypes
from contextlib import ExitStack

import concourse.bass as bass
import concourse.mybir as mybir
from concourse.bass_utils import run_bass_kernel_spmd

F32 = mybir.dt.float32
BF16 = mybir.dt.bfloat16
I32 = mybir.dt.int32
AF = mybir.ActivationFunctionType
ALU = mybir.AluOpType
AX = mybir.AxisListType

class Ser:

    def __init__(self, eng, sem):
        self._e = eng
        self._s = sem
        self._n = 0
        self._last = None

    def wait_ge(self, sem, val):
        return self._e.wait_ge(sem, val)

    def __getattr__(self, name):
        fn = getattr(self._e, name)

        def call(*a, _inc=None, **kw):
            if self._last is not None:
                self._e.wait_ge(*self._last)
            inst = fn(*a, **kw)
            if _inc is not None:
                sem, tgt = _inc
                inst.then_inc(sem, 1)
                self._last = (sem, tgt)
            else:
                self._n += 1
                inst.then_inc(self._s, 1)
                self._last = (self._s, self._n)
            return inst

        return call


NCORES = 8
DM = 1024
TPC = 32768 // NCORES
NT = TPC // 128
KC = 8


def _hadamard_bf16_packed():
    h = np.array([[1.0]], dtype=np.float64)
    while h.shape[0] < DM // 2:
        h = np.block([[h, h], [h, -h]])
    h = (h / 32.0).astype(np.float32)
    hp = h.reshape(4, 128, DM // 2).transpose(1, 0, 2).reshape(128, 4 * (DM // 2))
    return hp.astype(ml_dtypes.bfloat16)


def build_nc(nt=NT):
    tpc = nt * 128
    nc = bass.Bass("TRN2", target_bir_lowering=False)
    x_hbm = nc.dram_tensor("x", [tpc, DM], F32, kind="ExternalInput").ap()
    w1_hbm = nc.dram_tensor("w1", [DM, DM], F32, kind="ExternalInput").ap()
    w23_hbm = nc.dram_tensor("w23", [DM, 32], F32, kind="ExternalInput").ap()
    b1_hbm = nc.dram_tensor("b1r", [1, DM], F32, kind="ExternalInput").ap()
    b23_hbm = nc.dram_tensor("b23r", [1, 32], F32, kind="ExternalInput").ap()
    wf_hbm = nc.dram_tensor("wf", [DM, DM], F32, kind="ExternalInput").ap()
    h_hbm = nc.dram_tensor("hpk", [128, 4 * (DM // 2)], BF16, kind="ExternalInput").ap()
    identbf_hbm = nc.dram_tensor("identbf", [128, 128], BF16, kind="ExternalInput").ap()
    identf_hbm = nc.dram_tensor("identf", [128, 128], F32, kind="ExternalInput").ap()
    onesbf_hbm = nc.dram_tensor("onesbf", [1, 128], BF16, kind="ExternalInput").ap()
    onesf_hbm = nc.dram_tensor("onesf", [1, 128], F32, kind="ExternalInput").ap()
    c160_hbm = nc.dram_tensor("c160", [128, 1], F32, kind="ExternalInput").ap()
    nc160_hbm = nc.dram_tensor("nc160", [128, 1], F32, kind="ExternalInput").ap()
    out_hbm = nc.dram_tensor("out", [tpc, DM], F32, kind="ExternalOutput").ap()

    with ExitStack() as es:
        S = lambda n, sh, dt: es.enter_context(nc.sbuf_tensor(n, sh, dt))
        SEM = lambda n: es.enter_context(nc.semaphore(n))

        W1c = [S(f"W1c{i}", [128, DM], BF16) for i in range(KC)]
        Hs = S("Hs", [128, 4 * (DM // 2)], BF16)
        W23s = S("W23s", [128, KC * 32], BF16)
        ternT = S("ternT", [128, KC * DM], BF16)
        b1s = S("b1s", [1, DM], BF16)
        b23s = S("b23s", [1, 32], BF16)
        onesbf = S("onesbf_s", [1, 128], BF16)
        onesf = S("onesf_s", [1, 128], F32)
        identbf = S("identbf_s", [128, 128], BF16)
        identf = S("identf_s", [128, 128], F32)
        c160s = S("c160_s", [128, 1], F32)
        nc160s = S("nc160_s", [128, 1], F32)
        wfstage = [S(f"wfst{i}", [128, DM], F32) for i in range(3)]
        cstage = [S("cst0", [128, DM], BF16)]
        ternbf = [S(f"tbf{i}", [128, DM], BF16) for i in range(3)]
        sw_all = S("sw_all", [128, KC], F32)
        swh_all = S("swh_all", [128, KC], F32)
        nswh_all = S("nswh_all", [128, KC], F32)
        absum = S("absum", [128, 1], F32)
        xbuf = [S(f"xb{i}", [128, DM], F32) for i in range(11)]
        xnbf = [S(f"xnb{i}", [128, DM], BF16) for i in range(8)]
        xnT = [S(f"xnT{i}", [128, DM], BF16) for i in range(3)]
        uT2 = [S(f"uT{i}", [128, 512], BF16) for i in range(4)]
        vT2 = [S(f"vT{i}", [128, 512], BF16) for i in range(4)]
        xqm = S("xqm", [128, DM], BF16)
        xqbf = S("xqb", [128, DM], BF16)
        xqT = [S(f"xqT{i}", [128, DM], BF16) for i in range(3)]
        esb = S("esb", [128, DM], F32)
        l2 = S("l2", [128, DM], F32)
        s2 = [S(f"s2_{i}", [128, DM], F32) for i in range(3)]
        delta = [S(f"del{i}", [128, DM], BF16) for i in range(6)]
        o3s = [S(f"o3s{i}", [128, DM], F32) for i in range(2)]
        t1buf = S("t1b", [128, DM], F32)
        t3buf = S("t3b", [128, DM], F32)
        t4buf = S("t4b", [128, DM], F32)
        obuf = [S(f"ob{i}", [128, DM], F32) for i in range(2)]
        dump_a = S("dumpa", [128, DM], BF16)
        prod = S("prod", [128, 16], F32)
        bcw = S("bcw", [128, 32], F32)
        mk1 = lambda n: [S(f"{n}{i}", [128, 1], F32) for i in range(2)]
        sumsq = mk1("ssq"); xsum = mk1("xsm"); mean = mk1("mean"); var = mk1("var")
        mk4 = lambda n: [S(f"{n}{i}", [128, 1], F32) for i in range(4)]
        rs = mk1("rs"); nmrs = mk1("nmrs"); sa = mk1("sa"); nsa = mk4("nsa")
        inv = mk1("inv"); dot = mk1("dot"); cc = mk4("cc")
        mx = S("mx", [128, 1], F32); mxc = S("mxc", [128, 1], F32); mxh = S("mxh", [128, 1], F32)
        msq = S("msq", [128, 1], F32); lnv = S("lnv", [128, 1], F32)

        psZ1 = es.enter_context(nc.psum_tensor("psZ1", [128, DM], F32))
        psXH = es.enter_context(nc.psum_tensor("psXH", [128, DM], F32))
        psO3x = [es.enter_context(nc.psum_tensor(f"psO3{i}", [128, DM], F32)) for i in range(2)]
        psO3 = psO3x[0]

        sLDa = [SEM(f"sLD{i}") for i in range(11)]
        sWS = SEM("sWS"); sWG = SEM("sWG"); sC160 = SEM("sC160")
        sWFa = [SEM(f"sWF{i}") for i in range(3)]
        sTXNa = [SEM(f"sTXN{i}") for i in range(3)]
        sTXQa = [SEM(f"sTXQ{i}") for i in range(3)]
        sSTa = [SEM(f"sST{i}") for i in range(2)]
        sSWD = SEM("sSWD")
        sPEZ = SEM("sPEZ"); sPEXH = SEM("sPEXH"); sPEO3 = SEM("sPEO3")
        sXNB = SEM("sXNB"); sEXP = SEM("sEXP"); sRND = SEM("sRND"); sSIG = SEM("sSIG")
        sSQ = SEM("sSQ"); sRS = SEM("sRS")
        sVAR = SEM("sVAR"); sNMRS = SEM("sNMRS"); sMX = SEM("sMX"); sO3S = SEM("sO3S")
        sXQB = SEM("sXQB"); sBCE = SEM("sBCE"); sVOUT = SEM("sVOUT")
        sTBF = SEM("sTBF"); sPTT = SEM("sPTT"); sTEV = SEM("sTEV"); sUVT = SEM("sUVT")
        sDT4 = SEM("sDT4"); sPEBC = SEM("sPEBC"); sPT1 = SEM("sPT1"); sPEXH1 = SEM("sPEXH1")
        sSELFD = SEM("sSELFD"); sSELFA = SEM("sSELFA"); sSELFG = SEM("sSELFG"); sSELFP = SEM("sSELFP")

        block = es.enter_context(nc.Block())

        W23s3 = W23s[:].rearrange("p (c n) -> p c n", c=KC)
        ternT3 = ternT[:].rearrange("p (c n) -> p c n", c=KC)
        psO3bf = psO3[:].bitcast(BF16)
        psO3bf3 = psO3bf[:, 0:DM].rearrange("p (c n) -> p c n", c=KC)

        @block.sync
        def _(sp):
            sp.dma_start(out=c160s[:], in_=c160_hbm[:]).then_inc(sC160, 16)
            sp.dma_start(out=nc160s[:], in_=nc160_hbm[:]).then_inc(sC160, 16)
            sp.dma_start(out=identbf[:], in_=identbf_hbm[:]).then_inc(sWS, 16)
            sp.dma_start(out=identf[:], in_=identf_hbm[:]).then_inc(sWS, 16)
            sp.dma_start(out=onesbf[:], in_=onesbf_hbm[:]).then_inc(sWS, 16)
            sp.dma_start(out=onesf[:], in_=onesf_hbm[:]).then_inc(sWS, 16)
            for t in range(min(5, nt)):
                sp.dma_start(out=xbuf[t % 11][:], in_=x_hbm[t * 128:(t + 1) * 128, :]).then_inc(sLDa[t % 11], 16)
            sp.dma_start(out=Hs[:], in_=h_hbm[:]).then_inc(sWS, 16)
            for c in range(-4, nt + 4):
                t = c + 5
                if 5 <= t < nt:
                    if t >= 11:
                        sp.wait_ge(sVOUT, t - 10)
                    sp.dma_start(out=xbuf[t % 11][:], in_=x_hbm[t * 128:(t + 1) * 128, :]).then_inc(sLDa[t % 11], 16)
                t = c + 2
                if 0 <= t < nt:
                    sp.wait_ge(sXNB, t + 1)
                    if t >= 3:
                        sp.wait_ge(sPEBC, t - 1)
                        sp.wait_ge(sUVT, t - 2)
                    sp.dma_start(out=xnT[t % 3][:].rearrange("p (c n) -> p c n", c=KC),
                                 in_=xnbf[t % 8][:].rearrange("p (c n) -> p c n", c=KC),
                                 transpose=True).then_inc(sTXNa[t % 3], 16)
                t = c - 1
                if 0 <= t < nt:
                    sp.wait_ge(sXQB, t + 1)
                    if t >= 3:
                        sp.wait_ge(sPEO3, t - 2)
                    sp.dma_start(out=xqT[t % 3][:].rearrange("p (c n) -> p c n", c=KC),
                                 in_=xqbf[:].rearrange("p (c n) -> p c n", c=KC),
                                 transpose=True).then_inc(sTXQa[t % 3], 16)

        @block.gpsimd
        def _(g):
            g.dma_start(out=W23s3, in_=w23_hbm.rearrange("(c p) n -> p c n", p=128)).then_inc(sWG, 16)
            g.dma_start(out=b1s[:], in_=b1_hbm[:]).then_inc(sWG, 16)
            g.dma_start(out=b23s[:], in_=b23_hbm[:]).then_inc(sWG, 16)

            def w1_load(ic):
                g.dma_start(out=W1c[ic][:],
                            in_=w1_hbm[ic * 128:(ic + 1) * 128, :]).then_inc(sWG, 16)
            def wf_load(oc):
                if oc >= 3:
                    g.wait_ge(sTBF, oc - 2)
                g.dma_start(out=wfstage[oc % 3][:], in_=wf_hbm[oc * 128:(oc + 1) * 128, :]).then_inc(sWFa[oc % 3], 16)

            for ic in range(KC):
                w1_load(ic)
            for k in range(KC):
                wf_load(k)
            gs = Ser(g, sSELFG)
            for c in range(-4, nt + 4):
                tu = c + 1
                if 0 <= tu < nt:
                    g.wait_ge(sTXNa[tu % 3], 16 * (tu // 3 + 1))
                    if tu >= 4:
                        g.wait_ge(sPEXH, tu - 3)
                    xT_ = xnT[tu % 3]
                    gs.tensor_tensor(uT2[tu % 4][:], xT_[:, 0:512], xT_[:, 512:DM], ALU.add)
                    gs.tensor_tensor(vT2[tu % 4][:], xT_[:, 0:512], xT_[:, 512:DM],
                                     ALU.subtract, _inc=(sUVT, tu + 1))
                w2 = c - 4
                if 0 <= w2 < nt:
                    g.wait_ge(sDT4, w2 + 1)
                    g.wait_ge(sO3S, w2 + 1)
                    g.wait_ge(sLDa[w2 % 11], 16 * (w2 // 11 + 1))
                    if w2 >= 2:
                        g.wait_ge(sSTa[w2 % 2], 16 * ((w2 - 2) // 2 + 1))
                    gs.tensor_tensor(t4buf[:], t3buf[:], o3s[w2 % 2][:], ALU.mult)
                    gs.tensor_tensor(obuf[w2 % 2][:], xbuf[w2 % 11][:], t4buf[:], ALU.add, _inc=(sVOUT, w2 + 1))
                    g.wait_ge(sVOUT, w2 + 1)
                    g.dma_start(out=out_hbm[w2 * 128:(w2 + 1) * 128, :], in_=obuf[w2 % 2][:]).then_inc(sSTa[w2 % 2], 16)
            g.wait_ge(sSTa[0], 16 * ((nt + 1) // 2))
            if nt >= 2:
                g.wait_ge(sSTa[1], 16 * (nt // 2))

        @block.tensor
        def _(pe):
            def prep_tt(ocs):
                for oc in ocs:
                    pe.wait_ge(sTBF, oc + 1)
                    if oc >= 1:
                        pe.wait_ge(sTEV, oc)
                    last = None
                    for ic in range(KC):
                        last = pe.transpose(psO3bf[:, ic * 128 + 0:ic * 128 + 128],
                                            ternbf[oc % 3][:, ic * 128:(ic + 1) * 128], identbf[:])
                    last.then_inc(sPTT, 1)

            for c in range(-4, nt + 4):
                t = c
                if 0 <= t < nt:
                    if t == 0:
                        pe.wait_ge(sWG, 16 * 11)
                        pe.wait_ge(sWS, 16 * 5)
                        pe.wait_ge(sC160, 32)
                    pe.wait_ge(sTXNa[t % 3], 16 * (t // 3 + 1))
                    pe.wait_ge(sBCE, t + 1)
                    xT = xnT[t % 3]
                    for ic in range(KC):
                        lh = xT[:, ic * 128:(ic + 1) * 128]
                        pe.matmul(psZ1[:, 0:512], lh, W1c[ic][:, 0:512], start=(ic == 0), stop=False)
                        pe.matmul(psZ1[:, 512:1024], lh, W1c[ic][:, 512:1024], start=(ic == 0), stop=False)
                    pe.matmul(psZ1[:, 0:512], onesbf[:], b1s[:, 0:512], start=False, stop=True)
                    pe.matmul(psZ1[:, 512:1024], onesbf[:], b1s[:, 512:1024], start=False, stop=True).then_inc(sPEZ, 1)
                if c == 0:
                    prep_tt(range(0, 5))
                if c == 1:
                    prep_tt(range(5, KC))
                u = c - 2
                if 0 <= u < nt:
                    if u == 0:
                        pe.wait_ge(sTEV, KC)
                    pe.wait_ge(sTXQa[u % 3], 16 * (u // 3 + 1))
                    if u >= 2:
                        pe.wait_ge(sO3S, u - 1)
                        pe.wait_ge(sSIG, u - 1)
                    ps = psO3x[u % 2]
                    qT = xqT[u % 3]
                    last = None
                    for ic in range(KC):
                        lh = qT[:, ic * 128:(ic + 1) * 128]
                        pe.matmul(ps[:, 0:512], lh, ternT[:, ic * DM:ic * DM + 512], start=(ic == 0), stop=(ic == KC - 1))
                        last = pe.matmul(ps[:, 512:1024], lh, ternT[:, ic * DM + 512:(ic + 1) * DM], start=(ic == 0), stop=(ic == KC - 1))
                    last.then_inc(sPEO3, 1)
                tn = c + 1
                if 0 <= tn < nt:
                    if tn == 0:
                        pe.wait_ge(sWG, 16 * 11)
                        pe.wait_ge(sWS, 16 * 5)
                        pe.wait_ge(sC160, 32)
                    else:
                        pe.wait_ge(sEXP, tn)
                    pe.wait_ge(sTXNa[tn % 3], 16 * (tn // 3 + 1))
                    xTn = xnT[tn % 3]
                    for ic in range(KC):
                        pe.matmul(psZ1[:, 0:32], xTn[:, ic * 128:(ic + 1) * 128],
                                  W23s[:, ic * 32:(ic + 1) * 32], start=(ic == 0), stop=False)
                    pe.matmul(psZ1[:, 0:32], onesbf[:], b23s[:], start=False, stop=True).then_inc(sPEBC, 1)
                if 0 <= t < nt:
                    if t >= 1:
                        pe.wait_ge(sRND, t)
                    pe.wait_ge(sUVT, t + 1)
                    last = None
                    for ic in range(4):
                        pe.matmul(psXH[:, 0:512], uT2[t % 4][:, ic * 128:(ic + 1) * 128],
                                  Hs[:, ic * 512:(ic + 1) * 512], start=(ic == 0), stop=(ic == 3))
                        last = pe.matmul(psXH[:, 512:1024], vT2[t % 4][:, ic * 128:(ic + 1) * 128],
                                         Hs[:, ic * 512:(ic + 1) * 512], start=(ic == 0), stop=(ic == 3))
                    last.then_inc(sPEXH, 1)

        @block.scalar
        def _(ac):
            ac = Ser(ac, sSELFA)

            def prep_evac(ocs):
                for oc in ocs:
                    ac.wait_ge(sPTT, oc + 1)
                    ac.copy(ternT3[:, :, oc * 128:(oc + 1) * 128], psO3bf3, _inc=(sTEV, oc + 1))

            ac.wait_ge(sC160, 32)
            for c in range(-4, nt + 4):
                v4 = c + 4
                if 0 <= v4 < nt:
                    ac.wait_ge(sLDa[v4 % 11], 16 * (v4 // 11 + 1))
                    ac.activation(dump_a[:], xbuf[v4 % 11][:], AF.Square,
                                  accum_out=sumsq[v4 % 2][:], _inc=(sSQ, v4 + 1))
                v3 = c + 3
                if 0 <= v3 < nt:
                    ac.wait_ge(sVAR, v3 + 1)
                    ac.activation(lnv[:], var[v3 % 2][:], AF.Ln)
                    ac.activation(rs[v3 % 2][:], lnv[:], AF.Exp, scale=-0.5, _inc=(sRS, v3 + 1))
                if c - 5 >= 1:
                    ac.wait_ge(sVOUT, c - 5)
                v = c + 2
                if 0 <= v < nt:
                    ac.wait_ge(sNMRS, v + 1)
                    ac.activation(xnbf[v % 8][:], xbuf[v % 11][:], AF.Identity,
                                  bias=nmrs[v % 2][:], scale=rs[v % 2][:], _inc=(sXNB, v + 1))
                t = c
                if 0 <= t < nt:
                    ac.wait_ge(sPEZ, t + 1)
                    ac.activation(esb[:], psZ1[:], AF.Exp, _inc=(sEXP, t + 1))
                    ac.activation(delta[t % 6][:], esb[:], AF.Ln, bias=1.0)
                u = c - 1
                if 0 <= u < nt:
                    ac.wait_ge(sMX, u + 1)
                    ac.activation(xqm[:], psXH[:], AF.Identity, scale=inv[u % 2][:],
                                  bias=c160s[:], _inc=(sRND, u + 1))
                    if u >= 1:
                        ac.wait_ge(sTXQa[(u - 1) % 3], 16 * ((u - 1) // 3 + 1))
                    ac.activation(xqbf[:], xqm[:], AF.Identity, bias=nc160s[:], _inc=(sXQB, u + 1))
                w_ = c - 3
                if 0 <= w_ < nt:
                    ac.wait_ge(sO3S, w_ + 1)
                    ac.activation(esb[:], psO3x[w_ % 2][:], AF.Exp, scale=nsa[w_ % 4][:])
                    ac.activation(l2[:], esb[:], AF.Ln, bias=1.0)
                    ac.activation(s2[w_ % 3][:], l2[:], AF.Exp, scale=-1.0, _inc=(sSIG, w_ + 1))
                if c == 0:
                    prep_evac(range(0, 5))
                if c == 1:
                    prep_evac(range(5, KC))

        @block.vector
        def _(dv):
            dv = Ser(dv, sSELFD)

            def prep_pre(ocs):
                for oc in ocs:
                    dv.wait_ge(sWFa[oc % 3], 16 * (oc // 3 + 1))
                    dv.tensor_reduce(absum[:], wfstage[oc % 3][:], AX.X, ALU.add, apply_absolute_value=True)
                    dv.tensor_scalar(sw_all[:, oc:oc + 1], absum[:], 1.0 / DM, 1e-5, ALU.mult, ALU.max)
                    dv.tensor_single_scalar(swh_all[:, oc:oc + 1], sw_all[:, oc:oc + 1], 0.5, ALU.mult)
                    dv.tensor_single_scalar(nswh_all[:, oc:oc + 1], sw_all[:, oc:oc + 1], -0.5, ALU.mult)

            def prep_cp(ocs):
                for oc in ocs:
                    dv.tensor_single_scalar(cstage[0][:], wfstage[oc % 3][:],
                                            nswh_all[:, oc:oc + 1], ALU.is_lt)
                    if oc >= 3:
                        dv.wait_ge(sPTT, oc - 2)
                    dv.scalar_tensor_tensor(ternbf[oc % 3][:], wfstage[oc % 3][:], swh_all[:, oc:oc + 1],
                                            cstage[0][:], ALU.is_gt, ALU.subtract)
                    dv.tensor_single_scalar(ternbf[oc % 3][:], ternbf[oc % 3][:], sw_all[:, oc:oc + 1],
                                            ALU.mult, _inc=(sTBF, oc + 1))

            for c in range(-4, nt + 4):
                if c == 0:
                    prep_cp([2]); prep_pre([4]); prep_cp([3]); prep_pre([5]); prep_cp([4]); prep_pre([6])
                if c == 1:
                    prep_cp([5]); prep_pre([7]); prep_cp([6]); prep_cp([7])
                u = c - 1
                if 0 <= u < nt:
                    dv.wait_ge(sPEXH, u + 1)
                    dv.tensor_reduce(mx[:], psXH[:], AX.X, ALU.max, apply_absolute_value=True)
                    dv.tensor_single_scalar(mxc[:], mx[:], 1e-5, ALU.max)
                    dv.tensor_single_scalar(sa[u % 2][:], mxc[:], 1.0 / 7.0, ALU.mult)
                    dv.tensor_single_scalar(nsa[u % 4][:], sa[u % 2][:], -1.0, ALU.mult)
                    dv.reciprocal(inv[u % 2][:], sa[u % 2][:])
                    dv.tensor_tensor(cc[u % 4][:], dot[u % 2][:], sa[u % 2][:], ALU.mult, _inc=(sMX, u + 1))
                v4 = c + 4
                if 0 <= v4 < nt:
                    dv.wait_ge(sLDa[v4 % 11], 16 * (v4 // 11 + 1))
                    dv.tensor_reduce(xsum[v4 % 2][:], xbuf[v4 % 11][:], AX.X, ALU.add)
                    dv.wait_ge(sSQ, v4 + 1)
                    dv.tensor_single_scalar(mean[v4 % 2][:], xsum[v4 % 2][:], 1.0 / DM, ALU.mult)
                    dv.tensor_tensor(msq[:], mean[v4 % 2][:], mean[v4 % 2][:], ALU.mult)
                    dv.tensor_single_scalar(msq[:], msq[:], -1e-5, ALU.add)
                    dv.scalar_tensor_tensor(var[v4 % 2][:], sumsq[v4 % 2][:], 1.0 / DM, msq[:],
                                            ALU.mult, ALU.subtract, _inc=(sVAR, v4 + 1))
                v3 = c + 3
                if 0 <= v3 < nt:
                    dv.wait_ge(sRS, v3 + 1)
                    dv.scalar_tensor_tensor(nmrs[v3 % 2][:], mean[v3 % 2][:], -1.0, rs[v3 % 2][:],
                                            ALU.mult, ALU.mult, _inc=(sNMRS, v3 + 1))
                td = c + 1
                if 0 <= td < nt:
                    dv.wait_ge(sPEBC, td + 1)
                    dv.tensor_single_scalar(bcw[:], psZ1[:, 0:32], 0.0, ALU.add, _inc=(sBCE, td + 1))
                    dv.tensor_tensor(prod[:], bcw[:, 0:16], bcw[:, 16:32], ALU.mult)
                    dv.tensor_reduce(dot[td % 2][:], prod[:], AX.X, ALU.add)

                w_ = c - 3
                if 0 <= w_ < nt:
                    dv.wait_ge(sPEO3, w_ + 1)
                    if w_ >= 2:
                        dv.wait_ge(sVOUT, w_ - 1)
                    dv.tensor_copy(o3s[w_ % 2][:], psO3x[w_ % 2][:], _inc=(sO3S, w_ + 1))
                w2 = c - 4
                if 0 <= w2 < nt:
                    dv.wait_ge(sSIG, w2 + 1)
                    dv.tensor_tensor(t1buf[:], delta[w2 % 6][:], s2[w2 % 3][:], ALU.mult)
                    if w2 >= 1:
                        dv.wait_ge(sVOUT, w2)
                    dv.scalar_tensor_tensor(t3buf[:], xnbf[w2 % 8][:], cc[w2 % 4][:], t1buf[:],
                                            ALU.mult, ALU.mult, _inc=(sDT4, w2 + 1))
                if c == -1:
                    prep_pre([0]); prep_pre([1]); prep_cp([0]); prep_pre([2]); prep_cp([1]); prep_pre([3])

    return nc


_CACHE = {}


def kernel(L0_attn_x, L0_attn_W1, L0_attn_b1, L0_attn_W2, L0_attn_b2,
           L0_attn_W3, L0_attn_b3, L0_attn_A, L0_ffn_weight):
    del L0_attn_A
    x = np.ascontiguousarray(np.asarray(L0_attn_x, dtype=np.float32).reshape(32768, DM))
    W1 = np.ascontiguousarray(np.asarray(L0_attn_W1, dtype=np.float32))
    W23 = np.ascontiguousarray(np.concatenate(
        [np.asarray(L0_attn_W2, dtype=np.float32), np.asarray(L0_attn_W3, dtype=np.float32)], axis=1))
    b1r = np.ascontiguousarray(np.asarray(L0_attn_b1, dtype=np.float32).reshape(1, DM))
    b23r = np.ascontiguousarray(np.concatenate(
        [np.asarray(L0_attn_b2, dtype=np.float32), np.asarray(L0_attn_b3, dtype=np.float32)]).reshape(1, 32))
    wf = np.ascontiguousarray(np.asarray(L0_ffn_weight, dtype=np.float32))

    if "nc" not in _CACHE:
        _CACHE["nc"] = build_nc()
        _CACHE["hpk"] = _hadamard_bf16_packed()
        _CACHE["identbf"] = np.eye(128, dtype=ml_dtypes.bfloat16)
        _CACHE["identf"] = np.eye(128, dtype=np.float32)
        _CACHE["onesbf"] = np.ones((1, 128), dtype=ml_dtypes.bfloat16)
        _CACHE["onesf"] = np.ones((1, 128), dtype=np.float32)
        _CACHE["c160"] = np.full((128, 1), 160.0, dtype=np.float32)
        _CACHE["nc160"] = np.full((128, 1), -160.0, dtype=np.float32)
    nc = _CACHE["nc"]

    common = dict(w1=W1, w23=W23, b1r=b1r, b23r=b23r, wf=wf, hpk=_CACHE["hpk"],
                  identbf=_CACHE["identbf"], identf=_CACHE["identf"],
                  onesbf=_CACHE["onesbf"], onesf=_CACHE["onesf"], c160=_CACHE["c160"],
                  nc160=_CACHE["nc160"])
    in_maps = [dict(common, x=np.ascontiguousarray(x[i * TPC:(i + 1) * TPC])) for i in range(NCORES)]
    res = run_bass_kernel_spmd(nc, in_maps, core_ids=list(range(NCORES)))
    out = np.concatenate([np.asarray(r["out"]) for r in res.results], axis=0)
    return out.reshape(2, 16384, DM)
